# version 12
# baseline (speedup 1.0000x reference)
import numpy as np
import ml_dtypes
from contextlib import ExitStack

import concourse.bass as bass
import concourse.mybir as mybir
from concourse.bass_utils import run_bass_kernel_spmd

F32 = mybir.dt.float32
BF16 = mybir.dt.bfloat16
I32 = mybir.dt.int32
AF = mybir.ActivationFunctionType
ALU = mybir.AluOpType

B, S, D = 2, 4096, 2048
NH, NKV, HD = 32, 8, 64
INTER = 5632
EPS = 1e-5
DC = D // 128
ICN = INTER // 128
NT = S // 128
NW = 8
NJ = 8
QW = 512
KVW = 128
TOK = 1024
SCALE = 0.125


def _build(dbg=False):
    nc = bass.Bass("TRN2", target_bir_lowering=False)

    NQ = 8
    x_fm = nc.dram_tensor("x_fm", [DC, 128, S], F32, kind="ExternalInput")
    x_own = nc.dram_tensor("x_own", [DC, 128, TOK], F32, kind="ExternalInput")
    wq_t = nc.dram_tensor("wq_t", [DC, 128, QW], F32, kind="ExternalInput")
    wkv_t = nc.dram_tensor("wkv_t", [DC, 128, 2 * KVW], F32, kind="ExternalInput")
    wo_t = nc.dram_tensor("wo_t", [DC, 128, D], F32, kind="ExternalInput")
    gw_t = nc.dram_tensor("gw_t", [ICN, 128, D], F32, kind="ExternalInput")
    uw_t = nc.dram_tensor("uw_t", [ICN, 128, D], F32, kind="ExternalInput")
    dw_t = nc.dram_tensor("dw_t", [DC, 128, INTER], F32, kind="ExternalInput")
    gs_t = nc.dram_tensor("gs_t", [128, ICN], F32, kind="ExternalInput")
    us_t = nc.dram_tensor("us_t", [128, ICN], F32, kind="ExternalInput")
    ds_t = nc.dram_tensor("ds_t", [128, DC], F32, kind="ExternalInput")
    tabs = nc.dram_tensor("tabs", [S, 512], F32, kind="ExternalInput")
    ident = nc.dram_tensor("ident", [128, 128], F32, kind="ExternalInput")
    masks = nc.dram_tensor("masks", [4, 128, 512], F32, kind="ExternalInput")
    own_off = nc.dram_tensor("own_off", [1, 8], I32, kind="ExternalInput")
    o = nc.dram_tensor("o", [TOK, D], F32, kind="ExternalOutput")
    cc_in = nc.dram_tensor("cc_in", [NQ, QW, 512], BF16)
    cc_out = nc.dram_tensor("cc_out", [NQ, D, 512], BF16)
    if dbg:
        d_qfm = nc.dram_tensor("d_qfm", [128, 4 * S], BF16, kind="ExternalOutput")
        d_kfm = nc.dram_tensor("d_kfm", [128, 2 * S], BF16, kind="ExternalOutput")
        d_v65 = nc.dram_tensor("d_v65", [128, NT * 130], BF16, kind="ExternalOutput")
        d_avfm = nc.dram_tensor("d_avfm", [128, 4 * S], BF16, kind="ExternalOutput")
        d_r1 = nc.dram_tensor("d_r1", [128, DC * TOK], F32, kind="ExternalOutput")
        d_xn2 = nc.dram_tensor("d_xn2", [128, DC * TOK], BF16, kind="ExternalOutput")
        d_hdn = nc.dram_tensor("d_hdn", [128, ICN * TOK], BF16, kind="ExternalOutput")
        d_r2 = nc.dram_tensor("d_r2", [128, DC * TOK], F32, kind="ExternalOutput")

    sv, sh = {}, {}
    ops = None
    st_root = ExitStack()

    def sem(name):
        if name not in sh:
            sh[name] = st_root.enter_context(nc.semaphore(name))
            sv[name] = 0
        return sh[name]

    def E(eng, method, *args, w=(), i=None, n=1, lowprec=None, **kwargs):
        waits = [(sem(s), v) for (s, v) in w if v > 0]
        if i is not None:
            sem(i)
            sv[i] += n
            tgt = (sh[i], n)
        else:
            tgt = None

        def f(e, method=method, args=args, kwargs=kwargs, waits=waits, tgt=tgt,
              lowprec=lowprec):
            for h, v in waits:
                e.wait_ge(h, v)
            if method is not None:
                if lowprec:
                    with nc.allow_low_precision(lowprec):
                        ins = getattr(e, method)(*args, **kwargs)
                else:
                    ins = getattr(e, method)(*args, **kwargs)
                if tgt is not None:
                    ins.then_inc(*tgt)
            elif tgt is not None:
                e.engine_nop().then_inc(*tgt)
        ops[eng].append(f)

    def new_phase():
        nonlocal ops
        ops = {"tensor": [], "vector": [], "scalar": [], "gpsimd": [], "sync": []}

    def flush(name):
        with nc.Block(name) as blk:
            for engname in ("tensor", "vector", "scalar", "gpsimd", "sync"):
                lst = ops[engname]
                if lst:
                    getattr(blk, engname)(lambda e, lst=lst: [f(e) for f in lst])
        new_phase()

    def dma(eng, dst, src, i, w=()):
        E(eng, "dma_start", w=w, i=i, n=16, out=dst, in_=src)
        return sv[i]

    new_phase()
    with st_root:
        sbg = lambda nm, shp, dt: st_root.enter_context(nc.sbuf_tensor(nm, shp, dt))
        identf = sbg("identf", [128, 128], F32)
        ones_col = sbg("ones_col", [128, 1], BF16)
        ones_row = sbg("ones_row", [1, 128], BF16)
        gs_sb = sbg("gs_sb", [128, ICN], F32)
        us_sb = sbg("us_sb", [128, ICN], F32)
        ds_sb = sbg("ds_sb", [128, DC], F32)
        offs = sbg("offs", [1, 8], I32)

        dma("sync", identf[:, :], ident[:, :], "dS")
        dma("sync", gs_sb[:, :], gs_t[:, :], "dS")
        dma("sync", us_sb[:, :], us_t[:, :], "dS")
        dma("sync", ds_sb[:, :], ds_t[:, :], "dS")
        dma("sync", offs[:, :], own_off[:, :], "dS")
        E("vector", "memset", ones_col[:, :], 1.0)
        E("vector", "memset", ones_row[:, :], 1.0)
        E("sync", None, w=[("dS", sv["dS"])])
        flush("b0")

        with ExitStack() as sA:
            sbA = lambda nm, shp, dt: sA.enter_context(nc.sbuf_tensor(nm, shp, dt))
            q_fm = sbA("q_fm", [128, 4 * S], BF16)
            k_fm = sbA("k_fm", [128, 2 * S], BF16)
            v65 = sbA("v65", [128, NT * 130], BF16)

            with ExitStack() as s1:
                sb1 = lambda nm, shp, dt: s1.enter_context(nc.sbuf_tensor(nm, shp, dt))
                ps1 = lambda nm, shp, dt: s1.enter_context(nc.psum_tensor(nm, shp, dt))
                wq_sb = sb1("wq_sb", [128, DC * QW], BF16)
                wkv_sb = sb1("wkv_sb", [128, DC * 2 * KVW], BF16)
                identb = sb1("identb", [128, 128], BF16)
                ones128 = sb1("ones128", [128, 128], BF16)
                xb = [sb1(f"xb{i}", [128, DC * 512], BF16) for i in range(2)]
                xsq = sb1("xsq", [128, DC * 512], BF16)
                tab = [sb1(f"tab{i}", [128, 4 * 512], BF16) for i in range(2)]
                csum_q_pad = sb1("csum_q_pad", [128, QW], BF16)
                csum_kv_pad = sb1("csum_kv_pad", [128, 2 * KVW], BF16)
                mneg_pad = sb1("mneg_pad", [128, 512], BF16)
                denb = sb1("denb", [128, 512], BF16)
                m_row = sb1("m_row", [1, 512], F32)
                msq = sb1("msq", [1, 512], F32)
                var = sb1("var", [1, 512], F32)
                lnv = sb1("lnv", [1, 512], F32)
                rstd_row = sb1("rstd_row", [1, 512], F32)
                rc = [sb1(f"rc{i}", [128, 4], F32) for i in range(2)]
                qs = [sb1(f"qs{i}", [128, QW], BF16) for i in range(2)]
                ks = [sb1(f"ks{i}", [128, KVW], BF16) for i in range(2)]
                qro = [sb1(f"qro{i}", [128, QW], BF16) for i in range(2)]
                kro = [sb1(f"kro{i}", [128, KVW], BF16) for i in range(2)]
                rta = sb1("rta", [128, 256], BF16)
                rtb = sb1("rtb", [128, 256], BF16)
                av_fm = sb1("av_fm", [128, 4 * S], BF16)
                masks_sb = sb1("masks_sb", [128, 4 * 512], BF16)
                p_sb = [sb1(f"p_sb{i}", [128, 512], BF16) for i in range(3)]
                bc_sb = sb1("bc_sb", [64, 512], F32)
                p_qkv = ps1("p_qkv", [128, 512], F32)
                p_trc = ps1("p_trc", [128, 704], BF16)
                st2 = [ps1(f"st2_{i}", [128, 512], F32) for i in range(3)]
                avp = [ps1(f"avp{i}", [65, 512], F32) for i in range(2)]
                bcp = ps1("bcp", [128, 512], F32)
                p_rT = p_trc.ap()[:, 640:648].bitcast(F32)

                xbv, tabv = [None] * NW, [None] * NW
                dma("gpsimd", wq_sb.ap().rearrange("p (a c) -> p a c", a=DC),
                    wq_t[:, :, :].transpose([1, 0, 2]), "dWQK")
                dma("gpsimd", wkv_sb.ap().rearrange("p (a c) -> p a c", a=DC),
                    wkv_t[:, :, :].transpose([1, 0, 2]), "dWQK")
                wqkv = ("dWQK", 32)
                dma("gpsimd", identb[:, :], ident[:, :], "dCst")
                mval = dma("gpsimd",
                           masks_sb.ap().rearrange("p (a c) -> p a c", a=4),
                           masks[:, :, :].transpose([1, 0, 2]), "dCst")

                def load_win(w, wait=()):
                    sx, stb = f"dXB{w % 2}", f"dTB{w % 2}"
                    v = dma(
                        "gpsimd", xb[w % 2].ap().rearrange("p (a c) -> p a c", a=DC),
                        x_fm[:, :, 512 * w:512 * (w + 1)].transpose([1, 0, 2]),
                        sx, w=wait)
                    xbv[w] = (sx, v)
                    v = dma(
                        "gpsimd", tab[w % 2].ap().rearrange("p (a c) -> p a c", a=4),
                        tabs.ap().rearrange("(a p) c -> p a c", p=128)[
                            :, 4 * w:4 * (w + 1), :], stb)
                    tabv[w] = (stb, v)

                load_win(0)
                load_win(1)

                E("vector", "memset", ones128[:, :], 1.0, i="kz")
                E("vector", "memset", k_fm[:, :], 0.0, i="kz")
                E("vector", "memset", csum_q_pad[:, :], 0.0, i="kz")
                E("vector", "memset", csum_kv_pad[:, :], 0.0, i="kz")
                E("vector", "memset", mneg_pad[:, :], 0.0, i="kz")
                E("vector", "memset", denb[:, :], 0.0, i="kz")
                KZ = sv["kz"]

                for dc in range(DC):
                    E("tensor", "matmul", p_qkv[:, 0:QW], ones128[:, :],
                      wq_sb[:, dc * QW:(dc + 1) * QW],
                      w=[wqkv, ("kz", KZ)] if dc == 0 else (),
                      start=(dc == 0), stop=(dc == DC - 1),
                      i="aSum" if dc == DC - 1 else None)
                E("scalar", "copy", csum_q_pad[0:1, :], p_qkv[0:1, 0:QW],
                  w=[("aSum", sv["aSum"]), ("kz", KZ)], i="aM")
                for dc in range(DC):
                    E("tensor", "matmul", p_qkv[:, 0:2 * KVW], ones128[:, :],
                      wkv_sb[:, dc * 2 * KVW:(dc + 1) * 2 * KVW],
                      w=[("aM", sv["aM"])] if dc == 0 else (),
                      start=(dc == 0), stop=(dc == DC - 1),
                      i="aSum" if dc == DC - 1 else None)
                E("scalar", "copy", csum_kv_pad[0:1, :], p_qkv[0:1, 0:2 * KVW],
                  w=[("aSum", sv["aSum"])], i="aM")
                E("vector", "memset",
                  v65.ap().rearrange("p (a c) -> p a c", c=65)[:, :, 64:65], 1.0,
                  i="aOnes")

                pc = 0
                mc = 0
                tidx = 0
                pair_hist = []
                task_list = []

                def emit_main(j, h, idx):
                    nonlocal pc, mc
                    pair, rh = h % 4, h // 4
                    qsl = q_fm[:, pair * S + j * 512:pair * S + j * 512 + 512]
                    nkb = 4 * j + 4
                    pend = []
                    for kb in range(nkb):
                        stp, pb = st2[pc % 3], p_sb[pc % 3]
                        wts = [("zEX", pc - 2), ("zAV", pc - 2)]
                        if kb == 0:
                            wts.append(("zAVN", idx - 1))
                            if h == 0:
                                wts += [("aC1", 4 * (j + 1)), ("aKE", 4 * (j + 1))]
                        E("tensor", "matmul", stp[:, :],
                          k_fm[:, rh * S + kb * 128:rh * S + kb * 128 + 128],
                          qsl, w=wts, start=True, stop=True, i="zSC")
                        E("scalar", "activation", pb[:, :], stp[:, :], AF.Exp,
                          scale=SCALE, w=[("zSC", pc + 1), ("zAV", pc - 2)],
                          i="zEX")
                        masked = kb >= 4 * j
                        if masked:
                            rel = kb - 4 * j
                            mwts = [("zEX", pc + 1)]
                            if mc == 0:
                                mwts.append(("dCst", mval))
                            E("vector", "tensor_mul", pb[:, :], pb[:, :],
                              masks_sb[:, rel * 512: rel * 512 + 512],
                              w=mwts, i="zPM")
                            mc += 1

                        def av_ops(kb=kb, pc=pc, masked=masked, mc=mc, pb=pb,
                                   idx=idx, nkb=nkb, kv=rh):
                            wts2 = [("zPM", mc)] if masked else [("zEX", pc + 1)]
                            E("tensor", "matmul", avp[idx % 2][:, :],
                              v65[:, kb * 130 + kv * 65: kb * 130 + kv * 65 + 65],
                              pb[:, :], w=wts2, start=(kb == 0),
                              stop=(kb == nkb - 1), i="zAV")
                        pend.append(av_ops)
                        if kb >= 2:
                            pend[kb - 2]()
                        pc += 1
                    pend[nkb - 2]()
                    pend[nkb - 1]()

                def emit_norm(j, h, idx, pdone):
                    buf = idx % 2
                    pair, rh = h % 4, h // 4
                    E("scalar", "copy", denb[0:1, :], avp[buf][64:65, :],
                      w=[("zAV", pdone), ("zBC", idx)], i="zDB",
                      lowprec="softmax denominator via bf16 broadcast "
                              "matmul; av output is bf16 anyway")
                    E("tensor", "matmul", bcp[:, :], ones128[:, :],
                      denb[:, :], w=[("zDB", idx + 1), ("zRC", idx)],
                      start=True, stop=True, i="zBC")
                    E("vector", "reciprocal", bc_sb[:, :], bcp[0:64, :],
                      w=[("zBC", idx + 1), ("zAVN", idx)], i="zRC")
                    E("vector", "tensor_mul",
                      av_fm[64 * rh:64 * rh + 64,
                            pair * S + j * 512:pair * S + j * 512 + 512],
                      avp[buf][0:64, :], bc_sb[:, :],
                      w=[("zRC", idx + 1)], i="zAVN")

                def run_task(j, h):
                    nonlocal tidx
                    emit_main(j, h, tidx)
                    pair_hist.append(pc)
                    task_list.append((j, h))
                    if tidx >= 1:
                        jp, hp = task_list[tidx - 1]
                        emit_norm(jp, hp, tidx - 1, pair_hist[tidx - 1])
                    tidx += 1

                for g in range(NT + 1):
                    w, tt = g // 4, g % 4
                    if g < NT and tt == 0:
                        E("gpsimd", "tensor_mul", xsq[:, :], xb[w % 2][:, :],
                          xb[w % 2][:, :], w=[xbv[w], ("aSqC", w)], i="aSq")
                        if w + 2 < NW:
                            load_win(w + 2, wait=[("aKm", 4 * (w + 1)),
                                                  ("aR2", 4 * (w + 1))])
                        for dc in range(DC):
                            wts = ()
                            if dc == 0:
                                wts = [xbv[w], ("aKE", 4 * w),
                                       ("aVar", w) if w else ("aM", sv["aM"])]
                            E("tensor", "matmul", p_qkv[:, :], ones128[:, :],
                              xb[w % 2][:, dc * 512:(dc + 1) * 512],
                              w=wts, start=(dc == 0), stop=(dc == DC - 1),
                              i="aSum" if dc == DC - 1 else None)
                        E("scalar", "mul", m_row[:, :], p_qkv[0:1, :], 1.0 / D,
                          w=[("aSum", sv["aSum"])])
                        E("scalar", "mul", mneg_pad[0:1, :], p_qkv[0:1, :],
                          -1.0 / D, i="aM")
                        for dc in range(DC):
                            E("tensor", "matmul", p_qkv[:, :], ones128[:, :],
                              xsq[:, dc * 512:(dc + 1) * 512],
                              w=[("aSq", w + 1), ("aM", sv["aM"])] if dc == 0 else (),
                              start=(dc == 0), stop=(dc == DC - 1),
                              i="aSqC" if dc == DC - 1 else None)
                        E("vector", "tensor_mul", msq[:, :], m_row[:, :],
                          m_row[:, :], w=[("aM", sv["aM"])])
                        E("vector", "tensor_scalar_sub", msq[:, :], msq[:, :],
                          float(EPS))
                        E("vector", "scalar_tensor_tensor", var[:, :],
                          p_qkv[0:1, :], 1.0 / D, msq[:, :], ALU.mult,
                          ALU.subtract, w=[("aSqC", w + 1)], i="aVar")
                        E("scalar", "activation", lnv[:, :], var[:, :], AF.Ln,
                          w=[("aVar", w + 1)], i="aLn")
                        E("scalar", "activation", rstd_row[:, :], lnv[:, :],
                          AF.Exp, scale=-0.5, w=[("aLn", sv["aLn"])], i="aRst")
                        E("scalar", "copy", rc[w % 2][:, :], p_rT,
                          w=[("aRT", w + 1)], i="aRC")

                    if g < NT:
                        xw = xb[w % 2]
                        for dc in range(DC):
                            wts = [("aM", sv["aM"]), ("aKE", g),
                                   ("aVar", w + 1)] if dc == 0 else ()
                            E("tensor", "matmul", p_qkv[:, :],
                              xw[:, dc * 512 + tt * 128: dc * 512 + tt * 128 + 128],
                              wq_sb[:, dc * QW:(dc + 1) * QW],
                              w=wts, start=(dc == 0), stop=False)
                        E("tensor", "matmul", p_qkv[:, :],
                          mneg_pad[:, tt * 128:tt * 128 + 128],
                          csum_q_pad[:, :], start=False, stop=True, i="aQm")
                        if tt == 0:
                            for kk in range(4):
                                E("tensor", "transpose",
                                  p_rT[:, kk:kk + 1],
                                  rstd_row[0:1, 128 * kk:128 * (kk + 1)],
                                  identf[0:1, 0:1],
                                  w=[("aRst", w + 1), ("aRC", w),
                                     ("aC1", 4 * w - 1)] if kk == 0 else (),
                                  i="aRT" if kk == 3 else None)
                        for dc in range(DC):
                            wts = [("aQE", g + 1)] if dc == 0 else ()
                            E("tensor", "matmul", p_qkv[:, 0:2 * KVW],
                              xw[:, dc * 512 + tt * 128: dc * 512 + tt * 128 + 128],
                              wkv_sb[:, dc * 2 * KVW:(dc + 1) * 2 * KVW],
                              w=wts, start=(dc == 0), stop=False)
                        E("tensor", "matmul", p_qkv[:, 0:2 * KVW],
                          mneg_pad[:, tt * 128:tt * 128 + 128],
                          csum_kv_pad[:, :], start=False, stop=True, i="aKm")
                        E("scalar", "activation", qs[g % 2][:, :], p_qkv[:, :],
                          AF.Copy, scale=rc[w % 2][:, tt:tt + 1],
                          w=[("aQm", g + 1), ("aRC", w + 1), ("aR1", g - 1)],
                          i="aQE")
                        E("scalar", "activation", ks[g % 2][:, :],
                          p_qkv[:, 0:KVW],
                          AF.Copy, scale=rc[w % 2][:, tt:tt + 1],
                          w=[("aKm", g + 1), ("aR2", g - 1), ("aOnes", 1)])
                        E("scalar", "activation",
                          v65.ap().rearrange("p (a c) -> p a c", c=65)[
                              :, 2 * g:2 * g + 2, 0:64],
                          p_qkv.ap()[:, KVW:2 * KVW].rearrange(
                              "p (a c) -> p a c", a=2),
                          AF.Copy, scale=rc[w % 2][:, tt:tt + 1], i="aKE")
                        tba = tab[w % 2].ap().rearrange("p (a c) -> p a c", a=4)
                        for (src, dst, nh, wname, iname, tprev) in (
                                (qs[g % 2], qro[g % 2], 8, "aQE", "aR1", "aT1"),
                                (ks[g % 2], kro[g % 2], 2, "aKE", "aR2", "aT2")):
                            cosv = tba[:, tt:tt + 1, 0:256].rearrange(
                                "p a (h c) -> p (a h) c", c=32)[:, 0:nh, :]
                            sinv = tba[:, tt:tt + 1, 256:512].rearrange(
                                "p a (h c) -> p (a h) c", c=32)[:, 0:nh, :]
                            s4 = src.ap().rearrange("p (h c two) -> p h c two",
                                                    h=nh, two=2)
                            x0 = s4[:, :, :, 0:1].rearrange("p h c two -> p h (c two)")
                            x1 = s4[:, :, :, 1:2].rearrange("p h c two -> p h (c two)")
                            dv = dst.ap().rearrange("p (h c) -> p h c", h=nh)
                            out0, out1 = dv[:, :, 0:32], dv[:, :, 32:64]
                            sca = rta.ap().rearrange("p (h c) -> p h c", c=32)[
                                :, 0:nh, :]
                            scb = rtb.ap().rearrange("p (h c) -> p h c", c=32)[
                                :, 0:nh, :]
                            E("vector", "tensor_mul", sca, x1, sinv,
                              w=[(wname, g + 1), tabv[w], (tprev, g - 1)])
                            E("vector", "tensor_mul", out0, x0, cosv)
                            E("vector", "tensor_sub", out0, out0, sca)
                            E("vector", "tensor_mul", scb, x0, sinv)
                            E("vector", "tensor_mul", out1, x1, cosv)
                            E("vector", "tensor_add", out1, out1, scb, i=iname)

                    if g >= 1:
                        gp = g - 1
                        for ch in range(4):
                            wts = ()
                            if ch == 0:
                                wts = [("aR1", gp + 1), ("aC1", gp),
                                       ("aRC", min((gp + 1) // 4, NW - 1) + 1)]
                                if gp == 0:
                                    wts.append(("dCst", 16))
                            E("tensor", "transpose",
                              p_trc[:, ch * 128:(ch + 1) * 128],
                              qro[gp % 2][:, ch * 128:(ch + 1) * 128], identb[:, :],
                              w=wts, i="aT1" if ch == 3 else None)
                        E("tensor", "transpose", p_trc[:, 512:640],
                          kro[gp % 2][:, :], identb[:, :],
                          w=[("aR2", gp + 1)], i="aT2")
                        E("scalar", "copy",
                          q_fm.ap().rearrange("p (a c) -> p a c", a=4)[
                              :, :, gp * 128:(gp + 1) * 128],
                          p_trc.ap()[:, 0:512].rearrange(
                              "p (a c) -> p a c", a=4),
                          w=[("aT1", gp + 1), ("aT2", gp + 1)])
                        E("scalar", "copy", k_fm[0:64, gp * 128:(gp + 1) * 128],
                          p_trc[0:64, 512:640])
                        E("scalar", "copy",
                          k_fm[64:128, S + gp * 128:S + (gp + 1) * 128],
                          p_trc[64:128, 512:640], i="aC1")

                    if g < NT and w >= 1:
                        run_task(w - 1, 2 * tt)
                        run_task(w - 1, 2 * tt + 1)

                for h in range(8):
                    run_task(NJ - 1, h)
                jp, hp = task_list[-1]
                emit_norm(jp, hp, tidx - 1, pair_hist[-1])

                if dbg:
                    dma("sync", d_qfm.ap(), q_fm.ap(), "dS")
                    dma("sync", d_kfm.ap(), k_fm.ap(), "dS")
                    dma("sync", d_v65.ap(), v65.ap(), "dS")
                    dma("sync", d_avfm.ap(), av_fm.ap(), "dS")
                    E("sync", None, w=[("dS", sv["dS"])])
                    flush("dump12")

                for q in range(NQ):
                    dma("sync",
                        cc_in.ap()[q].rearrange("(a p) t -> p a t", p=128),
                        av_fm.ap().rearrange("p (a t) -> p a t", a=4)[
                            :, :, 512 * q:512 * (q + 1)], "dCC",
                        w=[("zAVN", 8 * (q + 1))])
                    E("gpsimd", "collective_compute", "AllGather", ALU.bypass,
                      [[0, 1, 2, 3], [4, 5, 6, 7]],
                      [cc_in.ap()[q].opt()], [cc_out.ap()[q].opt()],
                      w=[("dCC", 16 * (q + 1))], i="cc")

        with ExitStack() as s3o:
            sb3o = lambda nm, shp, dt: s3o.enter_context(nc.sbuf_tensor(nm, shp, dt))
            r = sb3o("r", [128, DC * TOK], F32)
            xn2 = sb3o("xn2", [128, DC * TOK], BF16)
            gw_sb = [sb3o(f"gw_sb{i}", [128, D], BF16) for i in range(2)]
            uw_sb = [sb3o(f"uw_sb{i}", [128, D], BF16) for i in range(2)]

            with ExitStack() as s3a:
                sb3 = lambda nm, shp, dt: s3a.enter_context(nc.sbuf_tensor(nm, shp, dt))
                ps3 = lambda nm, shp, dt: s3a.enter_context(nc.psum_tensor(nm, shp, dt))
                av_own = sb3("av_own", [128, DC * TOK], BF16)
                r1sq = sb3("r1sq", [128, DC * 512], BF16)
                wo_sb = [sb3(f"wo_sb{i}", [128, D], BF16) for i in range(2)]
                m2r = sb3("m2r", [1, 512], F32)
                msq2 = sb3("msq2", [1, 512], F32)
                var2 = sb3("var2", [1, 512], F32)
                lnv2 = sb3("lnv2", [1, 512], F32)
                rstd2 = sb3("rstd2", [1, 512], F32)
                rstd2b = sb3("rstd2b", [1, 512], BF16)
                mr_b = sb3("mr_b", [1, 512], BF16)
                b_rstd = sb3("b_rstd", [128, 512], F32)
                b_mr = sb3("b_mr", [128, 512], F32)
                xtmp = sb3("xtmp", [128, 512], F32)
                p_wo = [ps3(f"p_wo{i}", [128, 512], F32) for i in range(2)]
                p_st2 = ps3("p_st2", [1, 512], F32)
                p_bc1 = ps3("p_bc1", [128, 512], F32)
                p_bc2 = ps3("p_bc2", [128, 512], F32)

                rv = dma("sync", r.ap().rearrange("p (a t) -> p a t", a=DC),
                         x_own[:, :, :].transpose([1, 0, 2]), "dS")
                wov = [None] * DC
                wov[0] = ("dWO0", dma("gpsimd", wo_sb[0][:, :], wo_t[0, :, :],
                                      "dWO0", w=[("dCC", 64)]))
                wov[1] = ("dWO1", dma("gpsimd", wo_sb[1][:, :], wo_t[1, :, :],
                                      "dWO1"))
                def mk_dyn(c, h, eng):
                    def dyn_read(g, c=c, h=h):
                        g.wait_ge(sem("cc"), NQ)
                        with g.register(f"ro{c}_{h}") as ro:
                            g.reg_load(ro, offs[:1, 2 * c + h:2 * c + h + 1])
                            vv = cc_out.ap()[0].rearrange(
                                "(a p) t -> p a t", p=128)[:, 4 * c:4 * c + 4, :]
                            g.dma_start(
                                out=av_own.ap().rearrange(
                                    "p (a t) -> p a t", a=DC)[
                                    :, 4 * c:4 * c + 4, 512 * h:512 * h + 512],
                                in_=bass.AP(vv.tensor, ro, vv.ap)
                            ).then_inc(sem(f"dAV{c}"), 16)
                    ops[eng].append(dyn_read)
                    sem(f"dAV{c}")
                    sv[f"dAV{c}"] += 16
                for c, eng in ((0, "gpsimd"), (1, "sync"),
                               (2, "gpsimd"), (3, "sync")):
                    mk_dyn(c, 0, eng)
                    mk_dyn(c, 1, eng)

                gwv, uwv = [None] * ICN, [None] * ICN
                for i2 in range(2):
                    gwv[i2] = (f"dGW{i2}", dma("gpsimd", gw_sb[i2][:, :],
                                               gw_t[i2, :, :], f"dGW{i2}"))
                    uwv[i2] = (f"dUW{i2}", dma("gpsimd", uw_sb[i2][:, :],
                                               uw_t[i2, :, :], f"dUW{i2}"))
                for ohc in range(DC):
                    if ohc + 2 < DC:
                        sname = f"dWO{ohc % 2}"
                        wov[ohc + 2] = (sname, dma(
                            "gpsimd", wo_sb[ohc % 2][:, :], wo_t[ohc + 2, :, :],
                            sname, w=[("bWOm", 2 * (ohc + 1))]))
                    for th in range(2):
                        k = 2 * ohc + th
                        for kc in range(DC):
                            wts = [wov[ohc], ("bWOE", k - 1)] if kc == 0 else ()
                            if k == 0 and kc % 4 == 0:
                                wts = list(wts) + [(f"dAV{kc // 4}", 32)]
                            E("tensor", "matmul", p_wo[k % 2][:, :],
                              wo_sb[ohc % 2][:, kc * 128:(kc + 1) * 128],
                              av_own[:, kc * TOK + th * 512: kc * TOK + th * 512 + 512],
                              w=wts, start=(kc == 0), stop=(kc == DC - 1),
                              i="bWOm" if kc == DC - 1 else None)
                        rsl = r[:, ohc * TOK + th * 512: ohc * TOK + th * 512 + 512]
                        E("vector", "tensor_add", rsl, p_wo[k % 2][:, :], rsl,
                          w=[("bWOm", k + 1), ("dS", rv)], i="bWOE")
                        E("vector", "tensor_copy",
                          xn2[:, ohc * TOK + th * 512: ohc * TOK + th * 512 + 512],
                          rsl, w=[("bWOE", k + 1)], i="bR1B")

                for th in range(2):
                    for dc in range(DC):
                        wts = [("bR1B", 32), ("bS5", th)] if dc == 0 else ()
                        E("tensor", "matmul", p_st2[:, :], ones_col[:, :],
                          xn2[:, dc * TOK + th * 512: dc * TOK + th * 512 + 512],
                          w=wts, start=(dc == 0), stop=(dc == DC - 1),
                          i="bS1" if dc == DC - 1 else None)
                    E("scalar", "mul", m2r[:, :], p_st2[:, :], 1.0 / D,
                      w=[("bS1", th + 1)], i="bS2")
                    E("scalar", "activation",
                      r1sq.ap().rearrange("p (a c) -> p a c", a=DC),
                      xn2.ap().rearrange("p (a c) -> p a c", a=DC)[
                          :, :, th * 512:(th + 1) * 512],
                      AF.Square, w=[("bR1B", 32)], i="bSq")
                    for dc in range(DC):
                        wts = [("bSq", th + 1), ("bS2", th + 1)] if dc == 0 else ()
                        E("tensor", "matmul", p_st2[:, :], ones_col[:, :],
                          r1sq[:, dc * 512:(dc + 1) * 512],
                          w=wts, start=(dc == 0), stop=(dc == DC - 1),
                          i="bS3" if dc == DC - 1 else None)
                    E("vector", "tensor_mul", msq2[:, :], m2r[:, :], m2r[:, :],
                      w=[("bS2", th + 1)])
                    E("vector", "tensor_scalar_sub", msq2[:, :], msq2[:, :],
                      float(EPS))
                    E("vector", "scalar_tensor_tensor", var2[:, :], p_st2[:, :],
                      1.0 / D, msq2[:, :], ALU.mult, ALU.subtract,
                      w=[("bS3", th + 1)], i="bS4")
                    E("scalar", "activation", lnv2[:, :], var2[:, :], AF.Ln,
                      w=[("bS4", th + 1)], i="bLn2")
                    E("scalar", "activation", rstd2[:, :], lnv2[:, :], AF.Exp,
                      scale=-0.5, w=[("bLn2", sv["bLn2"])])
                    E("scalar", "copy", rstd2b[:, :], rstd2[:, :], i="bS5")
                    E("vector", "tensor_mul", mr_b[:, :], m2r[:, :], rstd2[:, :],
                      w=[("bS5", th + 1)], i="bS6")
                    E("tensor", "matmul", p_bc1[:, :], ones_row[0:1, :],
                      rstd2b[:, :], w=[("bS5", th + 1), ("bS6", th + 1),
                                       ("bS8", 2 * th)],
                      start=True, stop=True)
                    E("tensor", "matmul", p_bc2[:, :], ones_row[0:1, :],
                      mr_b[:, :], start=True, stop=True, i="bS7")
                    E("scalar", "copy", b_rstd[:, :], p_bc1[:, :],
                      w=[("bS7", th + 1)])
                    E("scalar", "copy", b_mr[:, :], p_bc2[:, :], i="bS8", n=2)
                    for dc in range(DC):
                        sl = slice(dc * TOK + th * 512, dc * TOK + th * 512 + 512)
                        wts = [("bS8", 2 * (th + 1)), ("bS3", th + 1)] \
                            if dc == 0 else ()
                        E("vector", "tensor_mul", xtmp[:, :], r[:, sl],
                          b_rstd[:, :], w=wts)
                        E("vector", "tensor_sub", xn2[:, sl], xtmp[:, :],
                          b_mr[:, :], i="bS9" if dc == DC - 1 else None)

                if dbg:
                    dma("sync", d_r1.ap(), r.ap(), "dS")
                    dma("sync", d_xn2.ap(), xn2.ap(), "dS")
                    E("sync", None, w=[("dS", sv["dS"])])
                    flush("dump3a")

            with ExitStack() as s3b:
                hdn = s3b.enter_context(nc.sbuf_tensor("hdn", [128, ICN * TOK], BF16))

                with ExitStack() as s3b1:
                    sb31 = lambda nm, shp, dt: s3b1.enter_context(
                        nc.sbuf_tensor(nm, shp, dt))
                    ps31 = lambda nm, shp, dt: s3b1.enter_context(
                        nc.psum_tensor(nm, shp, dt))
                    sg = [sb31(f"sg{i}", [128, 512], BF16) for i in range(2)]
                    p_g = [ps31(f"p_g{i}", [128, 512], F32) for i in range(2)]
                    p_u = [ps31(f"p_u{i}", [128, 512], F32) for i in range(2)]

                    for ic in range(ICN):
                        if ic + 2 < ICN:
                            sg_, su_ = f"dGW{ic % 2}", f"dUW{ic % 2}"
                            gwv[ic + 2] = (sg_, dma("gpsimd", gw_sb[ic % 2][:, :],
                                                    gw_t[ic + 2, :, :], sg_,
                                                    w=[("bGm", 4 * (ic + 1))]))
                            uwv[ic + 2] = (su_, dma("gpsimd", uw_sb[ic % 2][:, :],
                                                    uw_t[ic + 2, :, :], su_))
                        for half, psum, wsb, wval, bwait in (
                                (0, p_g[0], gw_sb, gwv, ("bSl", 2 * ic - 1)),
                                (1, p_g[1], gw_sb, gwv, ("bSl", 2 * ic)),
                                (2, p_u[0], uw_sb, uwv, ("bHd", 2 * ic - 1)),
                                (3, p_u[1], uw_sb, uwv, ("bHd", 2 * ic))):
                            toff = (half % 2) * 512
                            for dc in range(DC):
                                wts = ()
                                if dc == 0:
                                    wts = [wval[ic], bwait]
                                    if ic == 0 and half < 2:
                                        wts.append(("bS9", half + 1))
                                E("tensor", "matmul", psum[:, :],
                                  wsb[ic % 2][:, dc * 128:(dc + 1) * 128],
                                  xn2[:, dc * TOK + toff: dc * TOK + toff + 512],
                                  w=wts, start=(dc == 0), stop=(dc == DC - 1),
                                  i="bGm" if dc == DC - 1 else None)
                        for half in range(2):
                            E("scalar", "activation", sg[half][:, :],
                              p_g[half][:, :], AF.Silu,
                              scale=gs_sb[:, ic:ic + 1],
                              w=[("bGm", 4 * ic + 1 + half)], i="bSl")
                            E("vector", "scalar_tensor_tensor",
                              hdn[:, ic * TOK + half * 512:
                                  ic * TOK + half * 512 + 512],
                              p_u[half][:, :], us_sb[:, ic:ic + 1], sg[half][:, :],
                              ALU.mult, ALU.mult,
                              w=[("bGm", 4 * ic + 3 + half), ("bSl", 2 * ic + 1 + half)],
                              i="bHd")
                    if dbg:
                        dma("sync", d_hdn.ap(), hdn.ap(), "dS")
                        E("sync", None, w=[("dS", sv["dS"])])
                        flush("dump3b1")

                with ExitStack() as s3b2:
                    ps32 = lambda nm, shp, dt: s3b2.enter_context(
                        nc.psum_tensor(nm, shp, dt))
                    dw_sb = [xn2[:, i * INTER:(i + 1) * INTER] for i in range(2)]
                    p_d = [ps32(f"p_d{i}", [128, 512], F32) for i in range(2)]
                    dwv = [None] * DC
                    for i2 in range(2):
                        dwv[i2] = (f"dDW{i2}", dma(
                            "gpsimd", dw_sb[i2][:, :], dw_t[i2, :, :],
                            f"dDW{i2}",
                            w=[("bGm", 4 * ICN)] if i2 == 0 else ()))
                    for ohc in range(DC):
                        if ohc + 2 < DC:
                            sd_ = f"dDW{ohc % 2}"
                            dwv[ohc + 2] = (sd_, dma("gpsimd", dw_sb[ohc % 2][:, :],
                                                     dw_t[ohc + 2, :, :], sd_,
                                                     w=[("bDm", 2 * (ohc + 1))]))
                        for half in range(2):
                            for ic in range(ICN):
                                wts = ()
                                if ic == 0:
                                    wts = [dwv[ohc],
                                           ("bDE", 2 * ohc - 1 + half)]
                                    if ohc == 0 and half == 0:
                                        wts.append(("bHd", 2 * ICN))
                                E("tensor", "matmul", p_d[half][:, :],
                                  dw_sb[ohc % 2][:, ic * 128:(ic + 1) * 128],
                                  hdn[:, ic * TOK + half * 512:
                                      ic * TOK + half * 512 + 512],
                                  w=wts, start=(ic == 0), stop=(ic == ICN - 1),
                                  i="bDm" if ic == ICN - 1 else None)
                            rsl = r[:, ohc * TOK + half * 512:
                                    ohc * TOK + half * 512 + 512]
                            E("vector", "scalar_tensor_tensor", rsl,
                              p_d[half][:, :], ds_sb[:, ohc:ohc + 1], rsl,
                              ALU.mult, ALU.add,
                              w=[("bDm", 2 * ohc + 1 + half)], i="bDE")
                    if dbg:
                        dma("sync", d_r2.ap(), r.ap(), "dS")
                        E("sync", None, w=[("dS", sv["dS"])])
                        flush("dump3b2")

            with ExitStack() as s3c:
                sb3c = lambda nm, shp, dt: s3c.enter_context(
                    nc.sbuf_tensor(nm, shp, dt))
                ps3c = lambda nm, shp, dt: s3c.enter_context(
                    nc.psum_tensor(nm, shp, dt))
                out_tm = [sb3c(f"out_tm{i}", [128, D], F32) for i in range(2)]
                p_ot = [ps3c(f"p_ot{i}", [128, 512], F32) for i in range(2)]
                outv = [0] * 8
                for g8 in range(8):
                    t0 = 128 * g8
                    for b4 in range(4):
                        gb = 4 * g8 + b4
                        for q4 in range(4):
                            dc = 4 * b4 + q4
                            wts = ()
                            if q4 == 0:
                                wts = [("bOTC", gb - 1)]
                                if gb == 0:
                                    wts.append(("bDE", 32))
                            E("tensor", "transpose",
                              p_ot[gb % 2][:, q4 * 128:(q4 + 1) * 128],
                              r[:, dc * TOK + t0: dc * TOK + t0 + 128],
                              identf[:, :], w=wts,
                              i="bOT" if q4 == 3 else None)
                        cwts = [("bOT", gb + 1)]
                        if b4 == 0 and g8 >= 2:
                            cwts.append((f"dOUT{g8 % 2}", 16 * (g8 // 2)))
                        E("scalar", "copy",
                          out_tm[g8 % 2][:, b4 * 512:(b4 + 1) * 512],
                          p_ot[gb % 2][:, :], w=cwts, i="bOTC")
                    outv[g8] = dma("sync", o[t0:t0 + 128, :], out_tm[g8 % 2][:, :],
                                   f"dOUT{g8 % 2}", w=[("bOTC", 4 * (g8 + 1))])
                E("sync", None, w=[("dOUT0", outv[6]), ("dOUT1", outv[7])])
                flush("main")

    return nc



def _host_inputs(inputs):
    f32 = np.float32
    x = np.ascontiguousarray(np.asarray(inputs["L0_attn_x"], f32))
    wq = np.asarray(inputs["L0_attn_wq"], f32)
    wk = np.asarray(inputs["L0_attn_wk"], f32)
    wv = np.asarray(inputs["L0_attn_wv"], f32)
    wo = np.asarray(inputs["L0_attn_wo"], f32)
    gw = np.asarray(inputs["L0_ffn_gate_w"], f32)
    uw = np.asarray(inputs["L0_ffn_up_w"], f32)
    dw = np.asarray(inputs["L0_ffn_down_w"], f32)
    gs = np.asarray(inputs["L0_ffn_gate_s"], f32)
    us = np.asarray(inputs["L0_ffn_up_s"], f32)
    ds = np.asarray(inputs["L0_ffn_down_s"], f32)

    perm = [0, 4, 1, 5, 2, 6, 3, 7]
    wo_p = np.ascontiguousarray(
        wo.reshape(4, 8, HD, D)[:, perm].reshape(D, D))
    wo_t = np.ascontiguousarray(
        wo_p.reshape(DC, 128, DC, 128).transpose(2, 1, 0, 3).reshape(DC, 128, D))
    gw_t = np.ascontiguousarray(
        gw.T.reshape(DC, 128, ICN, 128).transpose(2, 1, 0, 3).reshape(ICN, 128, D))
    uw_t = np.ascontiguousarray(
        uw.T.reshape(DC, 128, ICN, 128).transpose(2, 1, 0, 3).reshape(ICN, 128, D))
    dw_t = np.ascontiguousarray(
        dw.T.reshape(ICN, 128, DC, 128).transpose(2, 1, 0, 3).reshape(DC, 128, INTER))
    gs_t = np.ascontiguousarray(gs.reshape(ICN, 128).T)
    us_t = np.ascontiguousarray(us.reshape(ICN, 128).T)
    ds_t = np.ascontiguousarray(ds.reshape(DC, 128).T)

    j = np.arange(0, HD, 2, dtype=np.float64) / HD
    inv = 1.0 / (10000.0 ** j)
    ang = np.arange(S, dtype=np.float64)[:, None] * inv[None, :]
    cos = np.cos(ang).astype(f32)
    sin = np.sin(ang).astype(f32)
    tabs = np.ascontiguousarray(
        np.concatenate([np.tile(cos, (1, 8)), np.tile(sin, (1, 8))], axis=1))
    identm = np.eye(128, dtype=f32)
    qi = np.arange(512)
    kp = np.arange(128)
    masks = np.stack([(qi[None, :] >= (128 * rel + kp[:, None])).astype(f32)
                      for rel in range(4)], axis=0)
    masks = np.ascontiguousarray(masks)

    in_maps = []
    for c in range(8):
        b, gidx = c // 4, c % 4
        xfm = np.ascontiguousarray(x[b].T).reshape(DC, 128, S)
        xo = np.ascontiguousarray(xfm[:, :, TOK * gidx:TOK * (gidx + 1)])
        wqc = np.ascontiguousarray(
            wq[:, QW * gidx:QW * (gidx + 1)].reshape(D, 8, HD)[:, perm]
            .reshape(D, QW)).reshape(DC, 128, QW)
        wkvc = np.ascontiguousarray(np.concatenate(
            [wk[:, KVW * gidx:KVW * (gidx + 1)],
             wv[:, KVW * gidx:KVW * (gidx + 1)]], axis=1)).reshape(DC, 128, 2 * KVW)
        in_maps.append({
            "x_fm": xfm, "x_own": xo, "wq_t": wqc, "wkv_t": wkvc,
            "wo_t": wo_t, "gw_t": gw_t, "uw_t": uw_t, "dw_t": dw_t,
            "gs_t": gs_t, "us_t": us_t, "ds_t": ds_t,
            "tabs": tabs, "ident": identm, "masks": masks,
            "own_off": np.array(
                [[(2 * gidx + (i % 2)) * D * 512 + (i // 2) * 4 * 128 * 512
                  for i in range(8)]],
                dtype=np.int32),
        })
    return in_maps


def kernel(**inputs):
    in_maps = _host_inputs(inputs)
    nc = _build(dbg=False)
    res = run_bass_kernel_spmd(nc, in_maps, core_ids=list(range(8)))
    out = np.empty((B, S, D), np.float32)
    for c in range(8):
        out[c // 4, TOK * (c % 4):TOK * (c % 4 + 1), :] = res.results[c]["o"]
    return out


# version 13 (final)
# speedup vs baseline: 1.0226x; 1.0226x over previous
import numpy as np
import ml_dtypes
from contextlib import ExitStack

import concourse.bass as bass
import concourse.mybir as mybir
from concourse.bass_utils import run_bass_kernel_spmd

F32 = mybir.dt.float32
BF16 = mybir.dt.bfloat16
I32 = mybir.dt.int32
AF = mybir.ActivationFunctionType
ALU = mybir.AluOpType

B, S, D = 2, 4096, 2048
NH, NKV, HD = 32, 8, 64
INTER = 5632
EPS = 1e-5
DC = D // 128
ICN = INTER // 128
NT = S // 128
NW = 8
NJ = 8
QW = 512
KVW = 128
TOK = 1024
SCALE = 0.125


def _build(dbg=False):
    nc = bass.Bass("TRN2", target_bir_lowering=False)

    NQ = 8
    x_fm = nc.dram_tensor("x_fm", [DC, 128, S], F32, kind="ExternalInput")
    x_own = nc.dram_tensor("x_own", [DC, 128, TOK], F32, kind="ExternalInput")
    wq_t = nc.dram_tensor("wq_t", [DC, 128, QW], F32, kind="ExternalInput")
    wkv_t = nc.dram_tensor("wkv_t", [DC, 128, 2 * KVW], F32, kind="ExternalInput")
    wo_t = nc.dram_tensor("wo_t", [DC, 128, D], F32, kind="ExternalInput")
    gw_t = nc.dram_tensor("gw_t", [ICN, 128, D], F32, kind="ExternalInput")
    uw_t = nc.dram_tensor("uw_t", [ICN, 128, D], F32, kind="ExternalInput")
    dw_t = nc.dram_tensor("dw_t", [DC, 128, INTER], F32, kind="ExternalInput")
    gs_t = nc.dram_tensor("gs_t", [128, ICN], F32, kind="ExternalInput")
    us_t = nc.dram_tensor("us_t", [128, ICN], F32, kind="ExternalInput")
    ds_t = nc.dram_tensor("ds_t", [128, DC], F32, kind="ExternalInput")
    tabs = nc.dram_tensor("tabs", [S, 512], F32, kind="ExternalInput")
    ident = nc.dram_tensor("ident", [128, 128], F32, kind="ExternalInput")
    masks = nc.dram_tensor("masks", [4, 128, 512], F32, kind="ExternalInput")
    own_off = nc.dram_tensor("own_off", [1, 8], I32, kind="ExternalInput")
    o = nc.dram_tensor("o", [TOK, D], F32, kind="ExternalOutput")
    cc_in = nc.dram_tensor("cc_in", [NQ, QW, 512], BF16)
    cc_out = nc.dram_tensor("cc_out", [NQ, D, 512], BF16)
    if dbg:
        d_qfm = nc.dram_tensor("d_qfm", [128, 4 * S], BF16, kind="ExternalOutput")
        d_kfm = nc.dram_tensor("d_kfm", [128, 2 * S], BF16, kind="ExternalOutput")
        d_v65 = nc.dram_tensor("d_v65", [128, NT * 130], BF16, kind="ExternalOutput")
        d_avfm = nc.dram_tensor("d_avfm", [128, 4 * S], BF16, kind="ExternalOutput")
        d_r1 = nc.dram_tensor("d_r1", [128, DC * TOK], F32, kind="ExternalOutput")
        d_xn2 = nc.dram_tensor("d_xn2", [128, DC * TOK], BF16, kind="ExternalOutput")
        d_hdn = nc.dram_tensor("d_hdn", [128, ICN * TOK], BF16, kind="ExternalOutput")
        d_r2 = nc.dram_tensor("d_r2", [128, DC * TOK], F32, kind="ExternalOutput")

    sv, sh = {}, {}
    ops = None
    st_root = ExitStack()

    def sem(name):
        if name not in sh:
            sh[name] = st_root.enter_context(nc.semaphore(name))
            sv[name] = 0
        return sh[name]

    def E(eng, method, *args, w=(), i=None, n=1, lowprec=None, **kwargs):
        waits = [(sem(s), v) for (s, v) in w if v > 0]
        if i is not None:
            sem(i)
            sv[i] += n
            tgt = (sh[i], n)
        else:
            tgt = None

        def f(e, method=method, args=args, kwargs=kwargs, waits=waits, tgt=tgt,
              lowprec=lowprec):
            for h, v in waits:
                e.wait_ge(h, v)
            if method is not None:
                if lowprec:
                    with nc.allow_low_precision(lowprec):
                        ins = getattr(e, method)(*args, **kwargs)
                else:
                    ins = getattr(e, method)(*args, **kwargs)
                if tgt is not None:
                    ins.then_inc(*tgt)
            elif tgt is not None:
                e.engine_nop().then_inc(*tgt)
        ops[eng].append(f)

    def new_phase():
        nonlocal ops
        ops = {"tensor": [], "vector": [], "scalar": [], "gpsimd": [], "sync": []}

    def flush(name):
        with nc.Block(name) as blk:
            for engname in ("tensor", "vector", "scalar", "gpsimd", "sync"):
                lst = ops[engname]
                if lst:
                    getattr(blk, engname)(lambda e, lst=lst: [f(e) for f in lst])
        new_phase()

    def dma(eng, dst, src, i, w=()):
        E(eng, "dma_start", w=w, i=i, n=16, out=dst, in_=src)
        return sv[i]

    new_phase()
    with st_root:
        sbg = lambda nm, shp, dt: st_root.enter_context(nc.sbuf_tensor(nm, shp, dt))
        identf = sbg("identf", [128, 128], F32)
        ones_col = sbg("ones_col", [128, 1], BF16)
        ones_row = sbg("ones_row", [1, 128], BF16)
        gs_sb = sbg("gs_sb", [128, ICN], F32)
        us_sb = sbg("us_sb", [128, ICN], F32)
        ds_sb = sbg("ds_sb", [128, DC], F32)
        offs = sbg("offs", [1, 8], I32)

        dma("sync", identf[:, :], ident[:, :], "dS")
        dma("sync", gs_sb[:, :], gs_t[:, :], "dS")
        dma("sync", us_sb[:, :], us_t[:, :], "dS")
        dma("sync", ds_sb[:, :], ds_t[:, :], "dS")
        dma("sync", offs[:, :], own_off[:, :], "dS")
        E("vector", "memset", ones_col[:, :], 1.0)
        E("vector", "memset", ones_row[:, :], 1.0)
        E("sync", None, w=[("dS", sv["dS"])])
        flush("b0")

        with ExitStack() as sA:
            sbA = lambda nm, shp, dt: sA.enter_context(nc.sbuf_tensor(nm, shp, dt))
            q_fm = sbA("q_fm", [128, 4 * S], BF16)
            k_fm = sbA("k_fm", [128, 2 * S], BF16)
            v65 = sbA("v65", [128, NT * 130], BF16)

            with ExitStack() as s1:
                sb1 = lambda nm, shp, dt: s1.enter_context(nc.sbuf_tensor(nm, shp, dt))
                ps1 = lambda nm, shp, dt: s1.enter_context(nc.psum_tensor(nm, shp, dt))
                wq_sb = sb1("wq_sb", [128, DC * QW], BF16)
                wkv_sb = sb1("wkv_sb", [128, DC * 2 * KVW], BF16)
                identb = sb1("identb", [128, 128], BF16)
                ones128 = sb1("ones128", [128, 128], BF16)
                xb = [sb1(f"xb{i}", [128, DC * 512], BF16) for i in range(2)]
                xsq = sb1("xsq", [128, DC * 512], BF16)
                tab = [sb1(f"tab{i}", [128, 4 * 512], BF16) for i in range(2)]
                csum_q_pad = sb1("csum_q_pad", [128, QW], BF16)
                csum_kv_pad = sb1("csum_kv_pad", [128, 2 * KVW], BF16)
                mneg_pad = sb1("mneg_pad", [128, 512], BF16)
                denb = sb1("denb", [128, 512], BF16)
                m_row = sb1("m_row", [1, 512], F32)
                msq = sb1("msq", [1, 512], F32)
                var = sb1("var", [1, 512], F32)
                lnv = sb1("lnv", [1, 512], F32)
                rstd_row = sb1("rstd_row", [1, 512], F32)
                rc = [sb1(f"rc{i}", [128, 4], F32) for i in range(2)]
                qs = [sb1(f"qs{i}", [128, QW], BF16) for i in range(2)]
                ks = [sb1(f"ks{i}", [128, KVW], BF16) for i in range(2)]
                qro = [sb1(f"qro{i}", [128, QW], BF16) for i in range(2)]
                kro = [sb1(f"kro{i}", [128, KVW], BF16) for i in range(2)]
                rta = sb1("rta", [128, 256], BF16)
                rtb = sb1("rtb", [128, 256], BF16)
                av_fm = sb1("av_fm", [128, 4 * S], BF16)
                masks_sb = sb1("masks_sb", [128, 4 * 512], BF16)
                p_sb = [sb1(f"p_sb{i}", [128, 512], BF16) for i in range(3)]
                bc_sb = sb1("bc_sb", [64, 512], F32)
                p_qkv = ps1("p_qkv", [128, 512], F32)
                p_trc = ps1("p_trc", [128, 704], BF16)
                st2 = [ps1(f"st2_{i}", [128, 512], F32) for i in range(3)]
                avp = [ps1(f"avp{i}", [65, 512], F32) for i in range(2)]
                bcp = ps1("bcp", [128, 512], F32)
                p_rT = p_trc.ap()[:, 640:648].bitcast(F32)

                xbv, tabv = [None] * NW, [None] * NW
                dma("gpsimd", wq_sb.ap().rearrange("p (a c) -> p a c", a=DC),
                    wq_t[:, :, :].transpose([1, 0, 2]), "dWQK")
                dma("gpsimd", wkv_sb.ap().rearrange("p (a c) -> p a c", a=DC),
                    wkv_t[:, :, :].transpose([1, 0, 2]), "dWQK")
                wqkv = ("dWQK", 32)
                dma("gpsimd", identb[:, :], ident[:, :], "dCst")
                mval = dma("gpsimd",
                           masks_sb.ap().rearrange("p (a c) -> p a c", a=4),
                           masks[:, :, :].transpose([1, 0, 2]), "dCst")

                def load_win(w, wait=()):
                    sx, stb = f"dXB{w % 2}", f"dTB{w % 2}"
                    v = dma(
                        "gpsimd", xb[w % 2].ap().rearrange("p (a c) -> p a c", a=DC),
                        x_fm[:, :, 512 * w:512 * (w + 1)].transpose([1, 0, 2]),
                        sx, w=wait)
                    xbv[w] = (sx, v)
                    v = dma(
                        "gpsimd", tab[w % 2].ap().rearrange("p (a c) -> p a c", a=4),
                        tabs.ap().rearrange("(a p) c -> p a c", p=128)[
                            :, 4 * w:4 * (w + 1), :], stb)
                    tabv[w] = (stb, v)

                load_win(0)
                load_win(1)

                E("vector", "memset", ones128[:, :], 1.0, i="kz")
                E("vector", "memset", k_fm[:, :], 0.0, i="kz")
                E("vector", "memset", csum_q_pad[:, :], 0.0, i="kz")
                E("vector", "memset", csum_kv_pad[:, :], 0.0, i="kz")
                E("vector", "memset", mneg_pad[:, :], 0.0, i="kz")
                E("vector", "memset", denb[:, :], 0.0, i="kz")
                KZ = sv["kz"]

                for dc in range(DC):
                    E("tensor", "matmul", p_qkv[:, 0:QW], ones128[:, :],
                      wq_sb[:, dc * QW:(dc + 1) * QW],
                      w=[wqkv, ("kz", KZ)] if dc == 0 else (),
                      start=(dc == 0), stop=(dc == DC - 1),
                      i="aSum" if dc == DC - 1 else None)
                E("scalar", "copy", csum_q_pad[0:1, :], p_qkv[0:1, 0:QW],
                  w=[("aSum", sv["aSum"]), ("kz", KZ)], i="aM")
                for dc in range(DC):
                    E("tensor", "matmul", p_qkv[:, 0:2 * KVW], ones128[:, :],
                      wkv_sb[:, dc * 2 * KVW:(dc + 1) * 2 * KVW],
                      w=[("aM", sv["aM"])] if dc == 0 else (),
                      start=(dc == 0), stop=(dc == DC - 1),
                      i="aSum" if dc == DC - 1 else None)
                E("scalar", "copy", csum_kv_pad[0:1, :], p_qkv[0:1, 0:2 * KVW],
                  w=[("aSum", sv["aSum"])], i="aM")
                E("vector", "memset",
                  v65.ap().rearrange("p (a c) -> p a c", c=65)[:, :, 64:65], 1.0,
                  i="aOnes")

                pc = 0
                mc = 0
                tidx = 0
                pair_hist = []
                task_list = []

                def emit_main(j, h, idx):
                    nonlocal pc, mc
                    pair, rh = h % 4, h // 4
                    qsl = q_fm[:, pair * S + j * 512:pair * S + j * 512 + 512]
                    nkb = 4 * j + 4
                    pend = []
                    for kb in range(nkb):
                        stp, pb = st2[pc % 3], p_sb[pc % 3]
                        wts = [("zEX", pc - 2), ("zAV", pc - 2)]
                        if kb == 0:
                            wts.append(("zAVN", idx - 1))
                            if h == 0:
                                wts += [("aC1", 4 * (j + 1)), ("aKE", 4 * (j + 1))]
                        E("tensor", "matmul", stp[:, :],
                          k_fm[:, rh * S + kb * 128:rh * S + kb * 128 + 128],
                          qsl, w=wts, start=True, stop=True, i="zSC")
                        E("scalar", "activation", pb[:, :], stp[:, :], AF.Exp,
                          scale=SCALE, w=[("zSC", pc + 1), ("zAV", pc - 2)],
                          i="zEX")
                        masked = kb >= 4 * j
                        if masked:
                            rel = kb - 4 * j
                            mwts = [("zEX", pc + 1)]
                            if mc == 0:
                                mwts.append(("dCst", mval))
                            E("vector", "tensor_mul", pb[:, :], pb[:, :],
                              masks_sb[:, rel * 512: rel * 512 + 512],
                              w=mwts, i="zPM")
                            mc += 1

                        def av_ops(kb=kb, pc=pc, masked=masked, mc=mc, pb=pb,
                                   idx=idx, nkb=nkb, kv=rh):
                            wts2 = [("zPM", mc)] if masked else [("zEX", pc + 1)]
                            E("tensor", "matmul", avp[idx % 2][:, :],
                              v65[:, kb * 130 + kv * 65: kb * 130 + kv * 65 + 65],
                              pb[:, :], w=wts2, start=(kb == 0),
                              stop=(kb == nkb - 1), i="zAV")
                        pend.append(av_ops)
                        if kb >= 1:
                            pend[kb - 1]()
                        pc += 1
                    pend[nkb - 1]()

                def emit_norm(j, h, idx, pdone):
                    buf = idx % 2
                    pair, rh = h % 4, h // 4
                    E("scalar", "copy", denb[0:1, :], avp[buf][64:65, :],
                      w=[("zAV", pdone), ("zBC", idx)], i="zDB",
                      lowprec="softmax denominator via bf16 broadcast "
                              "matmul; av output is bf16 anyway")
                    E("tensor", "matmul", bcp[:, :], ones128[:, :],
                      denb[:, :], w=[("zDB", idx + 1), ("zRC", idx)],
                      start=True, stop=True, i="zBC")
                    E("vector", "reciprocal", bc_sb[:, :], bcp[0:64, :],
                      w=[("zBC", idx + 1), ("zAVN", idx)], i="zRC")
                    E("vector", "tensor_mul",
                      av_fm[64 * rh:64 * rh + 64,
                            pair * S + j * 512:pair * S + j * 512 + 512],
                      avp[buf][0:64, :], bc_sb[:, :],
                      w=[("zRC", idx + 1)], i="zAVN")

                def run_task(j, h):
                    nonlocal tidx
                    emit_main(j, h, tidx)
                    pair_hist.append(pc)
                    task_list.append((j, h))
                    if tidx >= 1:
                        jp, hp = task_list[tidx - 1]
                        emit_norm(jp, hp, tidx - 1, pair_hist[tidx - 1])
                    tidx += 1

                for g in range(NT + 1):
                    w, tt = g // 4, g % 4
                    if g < NT and tt == 0:
                        E("gpsimd", "tensor_mul", xsq[:, :], xb[w % 2][:, :],
                          xb[w % 2][:, :], w=[xbv[w], ("aSqC", w)], i="aSq")
                        if w + 2 < NW:
                            load_win(w + 2, wait=[("aKm", 4 * (w + 1)),
                                                  ("aR2", 4 * (w + 1))])
                        for dc in range(DC):
                            wts = ()
                            if dc == 0:
                                wts = [xbv[w], ("aKE", 4 * w),
                                       ("aVar", w) if w else ("aM", sv["aM"])]
                            E("tensor", "matmul", p_qkv[:, :], ones128[:, :],
                              xb[w % 2][:, dc * 512:(dc + 1) * 512],
                              w=wts, start=(dc == 0), stop=(dc == DC - 1),
                              i="aSum" if dc == DC - 1 else None)
                        E("scalar", "mul", m_row[:, :], p_qkv[0:1, :], 1.0 / D,
                          w=[("aSum", sv["aSum"])])
                        E("scalar", "mul", mneg_pad[0:1, :], p_qkv[0:1, :],
                          -1.0 / D, i="aM")
                        for dc in range(DC):
                            E("tensor", "matmul", p_qkv[:, :], ones128[:, :],
                              xsq[:, dc * 512:(dc + 1) * 512],
                              w=[("aSq", w + 1), ("aM", sv["aM"])] if dc == 0 else (),
                              start=(dc == 0), stop=(dc == DC - 1),
                              i="aSqC" if dc == DC - 1 else None)
                        E("vector", "tensor_mul", msq[:, :], m_row[:, :],
                          m_row[:, :], w=[("aM", sv["aM"])])
                        E("vector", "tensor_scalar_sub", msq[:, :], msq[:, :],
                          float(EPS))
                        E("vector", "scalar_tensor_tensor", var[:, :],
                          p_qkv[0:1, :], 1.0 / D, msq[:, :], ALU.mult,
                          ALU.subtract, w=[("aSqC", w + 1)], i="aVar")
                        E("scalar", "activation", lnv[:, :], var[:, :], AF.Ln,
                          w=[("aVar", w + 1)], i="aLn")
                        E("scalar", "activation", rstd_row[:, :], lnv[:, :],
                          AF.Exp, scale=-0.5, w=[("aLn", sv["aLn"])], i="aRst")
                        E("scalar", "copy", rc[w % 2][:, :], p_rT,
                          w=[("aRT", w + 1)], i="aRC")

                    if g < NT:
                        xw = xb[w % 2]
                        for dc in range(DC):
                            wts = [("aM", sv["aM"]), ("aKE", g),
                                   ("aVar", w + 1)] if dc == 0 else ()
                            E("tensor", "matmul", p_qkv[:, :],
                              xw[:, dc * 512 + tt * 128: dc * 512 + tt * 128 + 128],
                              wq_sb[:, dc * QW:(dc + 1) * QW],
                              w=wts, start=(dc == 0), stop=False)
                        E("tensor", "matmul", p_qkv[:, :],
                          mneg_pad[:, tt * 128:tt * 128 + 128],
                          csum_q_pad[:, :], start=False, stop=True, i="aQm")
                        if tt == 0:
                            for kk in range(4):
                                E("tensor", "transpose",
                                  p_rT[:, kk:kk + 1],
                                  rstd_row[0:1, 128 * kk:128 * (kk + 1)],
                                  identf[0:1, 0:1],
                                  w=[("aRst", w + 1), ("aRC", w),
                                     ("aC1", 4 * w - 1)] if kk == 0 else (),
                                  i="aRT" if kk == 3 else None)
                        for dc in range(DC):
                            wts = [("aQE", g + 1)] if dc == 0 else ()
                            E("tensor", "matmul", p_qkv[:, 0:2 * KVW],
                              xw[:, dc * 512 + tt * 128: dc * 512 + tt * 128 + 128],
                              wkv_sb[:, dc * 2 * KVW:(dc + 1) * 2 * KVW],
                              w=wts, start=(dc == 0), stop=False)
                        E("tensor", "matmul", p_qkv[:, 0:2 * KVW],
                          mneg_pad[:, tt * 128:tt * 128 + 128],
                          csum_kv_pad[:, :], start=False, stop=True, i="aKm")
                        E("scalar", "activation", qs[g % 2][:, :], p_qkv[:, :],
                          AF.Copy, scale=rc[w % 2][:, tt:tt + 1],
                          w=[("aQm", g + 1), ("aRC", w + 1), ("aR1", g - 1)],
                          i="aQE")
                        E("scalar", "activation", ks[g % 2][:, :],
                          p_qkv[:, 0:KVW],
                          AF.Copy, scale=rc[w % 2][:, tt:tt + 1],
                          w=[("aKm", g + 1), ("aR2", g - 1), ("aOnes", 1)])
                        E("scalar", "activation",
                          v65.ap().rearrange("p (a c) -> p a c", c=65)[
                              :, 2 * g:2 * g + 2, 0:64],
                          p_qkv.ap()[:, KVW:2 * KVW].rearrange(
                              "p (a c) -> p a c", a=2),
                          AF.Copy, scale=rc[w % 2][:, tt:tt + 1], i="aKE")
                        tba = tab[w % 2].ap().rearrange("p (a c) -> p a c", a=4)
                        for (src, dst, nh, wname, iname, tprev) in (
                                (qs[g % 2], qro[g % 2], 8, "aQE", "aR1", "aT1"),
                                (ks[g % 2], kro[g % 2], 2, "aKE", "aR2", "aT2")):
                            cosv = tba[:, tt:tt + 1, 0:256].rearrange(
                                "p a (h c) -> p (a h) c", c=32)[:, 0:nh, :]
                            sinv = tba[:, tt:tt + 1, 256:512].rearrange(
                                "p a (h c) -> p (a h) c", c=32)[:, 0:nh, :]
                            s4 = src.ap().rearrange("p (h c two) -> p h c two",
                                                    h=nh, two=2)
                            x0 = s4[:, :, :, 0:1].rearrange("p h c two -> p h (c two)")
                            x1 = s4[:, :, :, 1:2].rearrange("p h c two -> p h (c two)")
                            dv = dst.ap().rearrange("p (h c) -> p h c", h=nh)
                            out0, out1 = dv[:, :, 0:32], dv[:, :, 32:64]
                            sca = rta.ap().rearrange("p (h c) -> p h c", c=32)[
                                :, 0:nh, :]
                            scb = rtb.ap().rearrange("p (h c) -> p h c", c=32)[
                                :, 0:nh, :]
                            E("vector", "tensor_mul", sca, x1, sinv,
                              w=[(wname, g + 1), tabv[w], (tprev, g - 1)])
                            E("vector", "tensor_mul", out0, x0, cosv)
                            E("vector", "tensor_sub", out0, out0, sca)
                            E("vector", "tensor_mul", scb, x0, sinv)
                            E("vector", "tensor_mul", out1, x1, cosv)
                            E("vector", "tensor_add", out1, out1, scb, i=iname)

                    if g >= 1:
                        gp = g - 1
                        for ch in range(4):
                            wts = ()
                            if ch == 0:
                                wts = [("aR1", gp + 1), ("aC1", gp),
                                       ("aRC", min((gp + 1) // 4, NW - 1) + 1)]
                                if gp == 0:
                                    wts.append(("dCst", 16))
                            E("tensor", "transpose",
                              p_trc[:, ch * 128:(ch + 1) * 128],
                              qro[gp % 2][:, ch * 128:(ch + 1) * 128], identb[:, :],
                              w=wts, i="aT1" if ch == 3 else None)
                        E("tensor", "transpose", p_trc[:, 512:640],
                          kro[gp % 2][:, :], identb[:, :],
                          w=[("aR2", gp + 1)], i="aT2")
                        E("scalar", "copy",
                          q_fm.ap().rearrange("p (a c) -> p a c", a=4)[
                              :, :, gp * 128:(gp + 1) * 128],
                          p_trc.ap()[:, 0:512].rearrange(
                              "p (a c) -> p a c", a=4),
                          w=[("aT1", gp + 1), ("aT2", gp + 1)])
                        E("scalar", "copy", k_fm[0:64, gp * 128:(gp + 1) * 128],
                          p_trc[0:64, 512:640])
                        E("scalar", "copy",
                          k_fm[64:128, S + gp * 128:S + (gp + 1) * 128],
                          p_trc[64:128, 512:640], i="aC1")

                    if g < NT and w >= 1:
                        run_task(w - 1, 2 * tt)
                        run_task(w - 1, 2 * tt + 1)

                for h in range(8):
                    run_task(NJ - 1, h)
                jp, hp = task_list[-1]
                emit_norm(jp, hp, tidx - 1, pair_hist[-1])

                if dbg:
                    dma("sync", d_qfm.ap(), q_fm.ap(), "dS")
                    dma("sync", d_kfm.ap(), k_fm.ap(), "dS")
                    dma("sync", d_v65.ap(), v65.ap(), "dS")
                    dma("sync", d_avfm.ap(), av_fm.ap(), "dS")
                    E("sync", None, w=[("dS", sv["dS"])])
                    flush("dump12")

                for q in range(NQ):
                    dma("sync",
                        cc_in.ap()[q].rearrange("(a p) t -> p a t", p=128),
                        av_fm.ap().rearrange("p (a t) -> p a t", a=4)[
                            :, :, 512 * q:512 * (q + 1)], "dCC",
                        w=[("zAVN", 8 * (q + 1))])
                    E("gpsimd", "collective_compute", "AllGather", ALU.bypass,
                      [[0, 1, 2, 3], [4, 5, 6, 7]],
                      [cc_in.ap()[q].opt()], [cc_out.ap()[q].opt()],
                      w=[("dCC", 16 * (q + 1))], i="cc")

        with ExitStack() as s3o:
            sb3o = lambda nm, shp, dt: s3o.enter_context(nc.sbuf_tensor(nm, shp, dt))
            r = sb3o("r", [128, DC * TOK], F32)
            xn2 = sb3o("xn2", [128, DC * TOK], BF16)
            gw_sb = [sb3o(f"gw_sb{i}", [128, D], BF16) for i in range(2)]
            uw_sb = [sb3o(f"uw_sb{i}", [128, D], BF16) for i in range(2)]

            with ExitStack() as s3a:
                sb3 = lambda nm, shp, dt: s3a.enter_context(nc.sbuf_tensor(nm, shp, dt))
                ps3 = lambda nm, shp, dt: s3a.enter_context(nc.psum_tensor(nm, shp, dt))
                av_own = sb3("av_own", [128, DC * TOK], BF16)
                r1sq = sb3("r1sq", [128, DC * 512], BF16)
                wo_sb = [sb3(f"wo_sb{i}", [128, D], BF16) for i in range(2)]
                m2r = sb3("m2r", [1, 512], F32)
                msq2 = sb3("msq2", [1, 512], F32)
                var2 = sb3("var2", [1, 512], F32)
                lnv2 = sb3("lnv2", [1, 512], F32)
                rstd2 = sb3("rstd2", [1, 512], F32)
                rstd2b = sb3("rstd2b", [1, 512], BF16)
                mr_b = sb3("mr_b", [1, 512], BF16)
                b_rstd = sb3("b_rstd", [128, 512], F32)
                b_mr = sb3("b_mr", [128, 512], F32)
                xtmp = sb3("xtmp", [128, 512], F32)
                p_wo = [ps3(f"p_wo{i}", [128, 512], F32) for i in range(2)]
                p_st2 = ps3("p_st2", [1, 512], F32)
                p_bc1 = ps3("p_bc1", [128, 512], F32)
                p_bc2 = ps3("p_bc2", [128, 512], F32)

                rv = dma("sync", r.ap().rearrange("p (a t) -> p a t", a=DC),
                         x_own[:, :, :].transpose([1, 0, 2]), "dS")
                wov = [None] * DC
                wov[0] = ("dWO0", dma("gpsimd", wo_sb[0][:, :], wo_t[0, :, :],
                                      "dWO0", w=[("dCC", 64)]))
                wov[1] = ("dWO1", dma("gpsimd", wo_sb[1][:, :], wo_t[1, :, :],
                                      "dWO1"))
                def mk_dyn(c, h, eng):
                    def dyn_read(g, c=c, h=h):
                        g.wait_ge(sem("cc"), NQ)
                        with g.register(f"ro{c}_{h}") as ro:
                            g.reg_load(ro, offs[:1, 2 * c + h:2 * c + h + 1])
                            vv = cc_out.ap()[0].rearrange(
                                "(a p) t -> p a t", p=128)[:, 4 * c:4 * c + 4, :]
                            g.dma_start(
                                out=av_own.ap().rearrange(
                                    "p (a t) -> p a t", a=DC)[
                                    :, 4 * c:4 * c + 4, 512 * h:512 * h + 512],
                                in_=bass.AP(vv.tensor, ro, vv.ap)
                            ).then_inc(sem(f"dAV{c}"), 16)
                    ops[eng].append(dyn_read)
                    sem(f"dAV{c}")
                    sv[f"dAV{c}"] += 16
                for c, eng in ((0, "gpsimd"), (1, "sync"),
                               (2, "gpsimd"), (3, "sync")):
                    mk_dyn(c, 0, eng)
                    mk_dyn(c, 1, eng)

                gwv, uwv = [None] * ICN, [None] * ICN
                for i2 in range(2):
                    gwv[i2] = (f"dGW{i2}", dma("gpsimd", gw_sb[i2][:, :],
                                               gw_t[i2, :, :], f"dGW{i2}"))
                    uwv[i2] = (f"dUW{i2}", dma("gpsimd", uw_sb[i2][:, :],
                                               uw_t[i2, :, :], f"dUW{i2}"))
                for ohc in range(DC):
                    if ohc + 2 < DC:
                        sname = f"dWO{ohc % 2}"
                        wov[ohc + 2] = (sname, dma(
                            "gpsimd", wo_sb[ohc % 2][:, :], wo_t[ohc + 2, :, :],
                            sname, w=[("bWOm", 2 * (ohc + 1))]))
                    for th in range(2):
                        k = 2 * ohc + th
                        for kc in range(DC):
                            wts = [wov[ohc], ("bWOE", k - 1)] if kc == 0 else ()
                            if k == 0 and kc % 4 == 0:
                                wts = list(wts) + [(f"dAV{kc // 4}", 32)]
                            E("tensor", "matmul", p_wo[k % 2][:, :],
                              wo_sb[ohc % 2][:, kc * 128:(kc + 1) * 128],
                              av_own[:, kc * TOK + th * 512: kc * TOK + th * 512 + 512],
                              w=wts, start=(kc == 0), stop=(kc == DC - 1),
                              i="bWOm" if kc == DC - 1 else None)
                        rsl = r[:, ohc * TOK + th * 512: ohc * TOK + th * 512 + 512]
                        E("vector", "tensor_add", rsl, p_wo[k % 2][:, :], rsl,
                          w=[("bWOm", k + 1), ("dS", rv)], i="bWOE")
                        E("vector", "tensor_copy",
                          xn2[:, ohc * TOK + th * 512: ohc * TOK + th * 512 + 512],
                          rsl, w=[("bWOE", k + 1)], i="bR1B")

                for th in range(2):
                    for dc in range(DC):
                        wts = [("bR1B", 32), ("bS5", th)] if dc == 0 else ()
                        E("tensor", "matmul", p_st2[:, :], ones_col[:, :],
                          xn2[:, dc * TOK + th * 512: dc * TOK + th * 512 + 512],
                          w=wts, start=(dc == 0), stop=(dc == DC - 1),
                          i="bS1" if dc == DC - 1 else None)
                    E("scalar", "mul", m2r[:, :], p_st2[:, :], 1.0 / D,
                      w=[("bS1", th + 1)], i="bS2")
                    E("scalar", "activation",
                      r1sq.ap().rearrange("p (a c) -> p a c", a=DC),
                      xn2.ap().rearrange("p (a c) -> p a c", a=DC)[
                          :, :, th * 512:(th + 1) * 512],
                      AF.Square, w=[("bR1B", 32)], i="bSq")
                    for dc in range(DC):
                        wts = [("bSq", th + 1), ("bS2", th + 1)] if dc == 0 else ()
                        E("tensor", "matmul", p_st2[:, :], ones_col[:, :],
                          r1sq[:, dc * 512:(dc + 1) * 512],
                          w=wts, start=(dc == 0), stop=(dc == DC - 1),
                          i="bS3" if dc == DC - 1 else None)
                    E("vector", "tensor_mul", msq2[:, :], m2r[:, :], m2r[:, :],
                      w=[("bS2", th + 1)])
                    E("vector", "tensor_scalar_sub", msq2[:, :], msq2[:, :],
                      float(EPS))
                    E("vector", "scalar_tensor_tensor", var2[:, :], p_st2[:, :],
                      1.0 / D, msq2[:, :], ALU.mult, ALU.subtract,
                      w=[("bS3", th + 1)], i="bS4")
                    E("scalar", "activation", lnv2[:, :], var2[:, :], AF.Ln,
                      w=[("bS4", th + 1)], i="bLn2")
                    E("scalar", "activation", rstd2[:, :], lnv2[:, :], AF.Exp,
                      scale=-0.5, w=[("bLn2", sv["bLn2"])])
                    E("scalar", "copy", rstd2b[:, :], rstd2[:, :], i="bS5")
                    E("vector", "tensor_mul", mr_b[:, :], m2r[:, :], rstd2[:, :],
                      w=[("bS5", th + 1)], i="bS6")
                    E("tensor", "matmul", p_bc1[:, :], ones_row[0:1, :],
                      rstd2b[:, :], w=[("bS5", th + 1), ("bS6", th + 1),
                                       ("bS8", 2 * th)],
                      start=True, stop=True)
                    E("tensor", "matmul", p_bc2[:, :], ones_row[0:1, :],
                      mr_b[:, :], start=True, stop=True, i="bS7")
                    E("scalar", "copy", b_rstd[:, :], p_bc1[:, :],
                      w=[("bS7", th + 1)])
                    E("scalar", "copy", b_mr[:, :], p_bc2[:, :], i="bS8", n=2)
                    for dc in range(DC):
                        sl = slice(dc * TOK + th * 512, dc * TOK + th * 512 + 512)
                        wts = [("bS8", 2 * (th + 1)), ("bS3", th + 1)] \
                            if dc == 0 else ()
                        E("vector", "tensor_mul", xtmp[:, :], r[:, sl],
                          b_rstd[:, :], w=wts)
                        E("vector", "tensor_sub", xn2[:, sl], xtmp[:, :],
                          b_mr[:, :], i="bS9" if dc == DC - 1 else None)

                if dbg:
                    dma("sync", d_r1.ap(), r.ap(), "dS")
                    dma("sync", d_xn2.ap(), xn2.ap(), "dS")
                    E("sync", None, w=[("dS", sv["dS"])])
                    flush("dump3a")

            with ExitStack() as s3b:
                hdn = s3b.enter_context(nc.sbuf_tensor("hdn", [128, ICN * TOK], BF16))

                with ExitStack() as s3b1:
                    sb31 = lambda nm, shp, dt: s3b1.enter_context(
                        nc.sbuf_tensor(nm, shp, dt))
                    ps31 = lambda nm, shp, dt: s3b1.enter_context(
                        nc.psum_tensor(nm, shp, dt))
                    sg = [sb31(f"sg{i}", [128, 512], BF16) for i in range(2)]
                    p_g = [ps31(f"p_g{i}", [128, 512], F32) for i in range(2)]
                    p_u = [ps31(f"p_u{i}", [128, 512], F32) for i in range(2)]

                    for ic in range(ICN):
                        if ic + 2 < ICN:
                            sg_, su_ = f"dGW{ic % 2}", f"dUW{ic % 2}"
                            gwv[ic + 2] = (sg_, dma("gpsimd", gw_sb[ic % 2][:, :],
                                                    gw_t[ic + 2, :, :], sg_,
                                                    w=[("bGm", 4 * (ic + 1))]))
                            uwv[ic + 2] = (su_, dma("gpsimd", uw_sb[ic % 2][:, :],
                                                    uw_t[ic + 2, :, :], su_))
                        for half, psum, wsb, wval, bwait in (
                                (0, p_g[0], gw_sb, gwv, ("bSl", 2 * ic - 1)),
                                (1, p_g[1], gw_sb, gwv, ("bSl", 2 * ic)),
                                (2, p_u[0], uw_sb, uwv, ("bHd", 2 * ic - 1)),
                                (3, p_u[1], uw_sb, uwv, ("bHd", 2 * ic))):
                            toff = (half % 2) * 512
                            for dc in range(DC):
                                wts = ()
                                if dc == 0:
                                    wts = [wval[ic], bwait]
                                    if ic == 0 and half < 2:
                                        wts.append(("bS9", half + 1))
                                E("tensor", "matmul", psum[:, :],
                                  wsb[ic % 2][:, dc * 128:(dc + 1) * 128],
                                  xn2[:, dc * TOK + toff: dc * TOK + toff + 512],
                                  w=wts, start=(dc == 0), stop=(dc == DC - 1),
                                  i="bGm" if dc == DC - 1 else None)
                        for half in range(2):
                            E("scalar", "activation", sg[half][:, :],
                              p_g[half][:, :], AF.Silu,
                              scale=gs_sb[:, ic:ic + 1],
                              w=[("bGm", 4 * ic + 1 + half)], i="bSl")
                            E("vector", "scalar_tensor_tensor",
                              hdn[:, ic * TOK + half * 512:
                                  ic * TOK + half * 512 + 512],
                              p_u[half][:, :], us_sb[:, ic:ic + 1], sg[half][:, :],
                              ALU.mult, ALU.mult,
                              w=[("bGm", 4 * ic + 3 + half), ("bSl", 2 * ic + 1 + half)],
                              i="bHd")
                    if dbg:
                        dma("sync", d_hdn.ap(), hdn.ap(), "dS")
                        E("sync", None, w=[("dS", sv["dS"])])
                        flush("dump3b1")

                with ExitStack() as s3b2:
                    ps32 = lambda nm, shp, dt: s3b2.enter_context(
                        nc.psum_tensor(nm, shp, dt))
                    dw_sb = [xn2[:, i * INTER:(i + 1) * INTER] for i in range(2)]
                    p_d = [ps32(f"p_d{i}", [128, 512], F32) for i in range(2)]
                    dwv = [None] * DC
                    for i2 in range(2):
                        dwv[i2] = (f"dDW{i2}", dma(
                            "gpsimd", dw_sb[i2][:, :], dw_t[i2, :, :],
                            f"dDW{i2}",
                            w=[("bGm", 4 * ICN)] if i2 == 0 else ()))
                    for ohc in range(DC):
                        if ohc + 2 < DC:
                            sd_ = f"dDW{ohc % 2}"
                            dwv[ohc + 2] = (sd_, dma("gpsimd", dw_sb[ohc % 2][:, :],
                                                     dw_t[ohc + 2, :, :], sd_,
                                                     w=[("bDm", 2 * (ohc + 1))]))
                        for half in range(2):
                            for ic in range(ICN):
                                wts = ()
                                if ic == 0:
                                    wts = [dwv[ohc],
                                           ("bDE", 2 * ohc - 1 + half)]
                                    if ohc == 0 and half == 0:
                                        wts.append(("bHd", 2 * ICN))
                                E("tensor", "matmul", p_d[half][:, :],
                                  dw_sb[ohc % 2][:, ic * 128:(ic + 1) * 128],
                                  hdn[:, ic * TOK + half * 512:
                                      ic * TOK + half * 512 + 512],
                                  w=wts, start=(ic == 0), stop=(ic == ICN - 1),
                                  i="bDm" if ic == ICN - 1 else None)
                            rsl = r[:, ohc * TOK + half * 512:
                                    ohc * TOK + half * 512 + 512]
                            E("vector", "scalar_tensor_tensor", rsl,
                              p_d[half][:, :], ds_sb[:, ohc:ohc + 1], rsl,
                              ALU.mult, ALU.add,
                              w=[("bDm", 2 * ohc + 1 + half)], i="bDE")
                    if dbg:
                        dma("sync", d_r2.ap(), r.ap(), "dS")
                        E("sync", None, w=[("dS", sv["dS"])])
                        flush("dump3b2")

            with ExitStack() as s3c:
                sb3c = lambda nm, shp, dt: s3c.enter_context(
                    nc.sbuf_tensor(nm, shp, dt))
                ps3c = lambda nm, shp, dt: s3c.enter_context(
                    nc.psum_tensor(nm, shp, dt))
                out_tm = [sb3c(f"out_tm{i}", [128, D], F32) for i in range(2)]
                p_ot = [ps3c(f"p_ot{i}", [128, 512], F32) for i in range(2)]
                outv = [0] * 8
                for g8 in range(8):
                    t0 = 128 * g8
                    for b4 in range(4):
                        gb = 4 * g8 + b4
                        for q4 in range(4):
                            dc = 4 * b4 + q4
                            wts = ()
                            if q4 == 0:
                                wts = [("bOTC", gb - 1)]
                                if gb == 0:
                                    wts.append(("bDE", 32))
                            E("tensor", "transpose",
                              p_ot[gb % 2][:, q4 * 128:(q4 + 1) * 128],
                              r[:, dc * TOK + t0: dc * TOK + t0 + 128],
                              identf[:, :], w=wts,
                              i="bOT" if q4 == 3 else None)
                        cwts = [("bOT", gb + 1)]
                        if b4 == 0 and g8 >= 2:
                            cwts.append((f"dOUT{g8 % 2}", 16 * (g8 // 2)))
                        E("scalar", "copy",
                          out_tm[g8 % 2][:, b4 * 512:(b4 + 1) * 512],
                          p_ot[gb % 2][:, :], w=cwts, i="bOTC")
                    outv[g8] = dma("sync", o[t0:t0 + 128, :], out_tm[g8 % 2][:, :],
                                   f"dOUT{g8 % 2}", w=[("bOTC", 4 * (g8 + 1))])
                E("sync", None, w=[("dOUT0", outv[6]), ("dOUT1", outv[7])])
                flush("main")

    return nc



def _host_inputs(inputs):
    f32 = np.float32
    x = np.ascontiguousarray(np.asarray(inputs["L0_attn_x"], f32))
    wq = np.asarray(inputs["L0_attn_wq"], f32)
    wk = np.asarray(inputs["L0_attn_wk"], f32)
    wv = np.asarray(inputs["L0_attn_wv"], f32)
    wo = np.asarray(inputs["L0_attn_wo"], f32)
    gw = np.asarray(inputs["L0_ffn_gate_w"], f32)
    uw = np.asarray(inputs["L0_ffn_up_w"], f32)
    dw = np.asarray(inputs["L0_ffn_down_w"], f32)
    gs = np.asarray(inputs["L0_ffn_gate_s"], f32)
    us = np.asarray(inputs["L0_ffn_up_s"], f32)
    ds = np.asarray(inputs["L0_ffn_down_s"], f32)

    perm = [0, 4, 1, 5, 2, 6, 3, 7]
    wo_p = np.ascontiguousarray(
        wo.reshape(4, 8, HD, D)[:, perm].reshape(D, D))
    wo_t = np.ascontiguousarray(
        wo_p.reshape(DC, 128, DC, 128).transpose(2, 1, 0, 3).reshape(DC, 128, D))
    gw_t = np.ascontiguousarray(
        gw.T.reshape(DC, 128, ICN, 128).transpose(2, 1, 0, 3).reshape(ICN, 128, D))
    uw_t = np.ascontiguousarray(
        uw.T.reshape(DC, 128, ICN, 128).transpose(2, 1, 0, 3).reshape(ICN, 128, D))
    dw_t = np.ascontiguousarray(
        dw.T.reshape(ICN, 128, DC, 128).transpose(2, 1, 0, 3).reshape(DC, 128, INTER))
    gs_t = np.ascontiguousarray(gs.reshape(ICN, 128).T)
    us_t = np.ascontiguousarray(us.reshape(ICN, 128).T)
    ds_t = np.ascontiguousarray(ds.reshape(DC, 128).T)

    j = np.arange(0, HD, 2, dtype=np.float64) / HD
    inv = 1.0 / (10000.0 ** j)
    ang = np.arange(S, dtype=np.float64)[:, None] * inv[None, :]
    cos = np.cos(ang).astype(f32)
    sin = np.sin(ang).astype(f32)
    tabs = np.ascontiguousarray(
        np.concatenate([np.tile(cos, (1, 8)), np.tile(sin, (1, 8))], axis=1))
    identm = np.eye(128, dtype=f32)
    qi = np.arange(512)
    kp = np.arange(128)
    masks = np.stack([(qi[None, :] >= (128 * rel + kp[:, None])).astype(f32)
                      for rel in range(4)], axis=0)
    masks = np.ascontiguousarray(masks)

    in_maps = []
    for c in range(8):
        b, gidx = c // 4, c % 4
        xfm = np.ascontiguousarray(x[b].T).reshape(DC, 128, S)
        xo = np.ascontiguousarray(xfm[:, :, TOK * gidx:TOK * (gidx + 1)])
        wqc = np.ascontiguousarray(
            wq[:, QW * gidx:QW * (gidx + 1)].reshape(D, 8, HD)[:, perm]
            .reshape(D, QW)).reshape(DC, 128, QW)
        wkvc = np.ascontiguousarray(np.concatenate(
            [wk[:, KVW * gidx:KVW * (gidx + 1)],
             wv[:, KVW * gidx:KVW * (gidx + 1)]], axis=1)).reshape(DC, 128, 2 * KVW)
        in_maps.append({
            "x_fm": xfm, "x_own": xo, "wq_t": wqc, "wkv_t": wkvc,
            "wo_t": wo_t, "gw_t": gw_t, "uw_t": uw_t, "dw_t": dw_t,
            "gs_t": gs_t, "us_t": us_t, "ds_t": ds_t,
            "tabs": tabs, "ident": identm, "masks": masks,
            "own_off": np.array(
                [[(2 * gidx + (i % 2)) * D * 512 + (i // 2) * 4 * 128 * 512
                  for i in range(8)]],
                dtype=np.int32),
        })
    return in_maps


def kernel(**inputs):
    in_maps = _host_inputs(inputs)
    nc = _build(dbg=False)
    res = run_bass_kernel_spmd(nc, in_maps, core_ids=list(range(8)))
    out = np.empty((B, S, D), np.float32)
    for c in range(8):
        out[c // 4, TOK * (c % 4):TOK * (c % 4 + 1), :] = res.results[c]["o"]
    return out
